# version 01 (final)
# baseline (speedup 1.0000x reference)
import numpy as np
import concourse.bass as bass
import concourse.mybir as mybir
from concourse.bass_utils import run_bass_kernel_spmd
from contextlib import ExitStack

F32 = mybir.dt.float32
AF = mybir.ActivationFunctionType
ALU = mybir.AluOpType

T = 4096
D = 2048
TOK = 1024
NCHD = 16
EPS_RMS0 = 1.1920929e-07
M0, M1, M2 = 12.0, 10.0, 5.0
SC0 = float(1.0 / np.sqrt(128.0))
SC1 = float(192.0 ** -0.5)
SC2 = float(1.0 / np.sqrt(128.0))
GROUPS = [[0, 1, 2, 3], [4, 5, 6, 7]]

ENGS = ("sy", "gp", "pe", "act", "dv")


class Prog:

    def __init__(self):
        self.q = {e: [] for e in ENGS}
        self.c = {e: 0 for e in ENGS}
        self.c["cc"] = 0

    def _fix(self, w):
        return tuple((s, self.c[s]) if s in ("sy", "gp") else (s, v)
                     for (s, v) in w if (s, v) is not None)

    def op(self, e, fname, *a, w=(), sem=None, n=None, **kw):
        if sem is None:
            sem = e
        if n is None:
            n = 16 if fname == "dma_start" else 1
        w = self._fix(x for x in w if x is not None)
        self.q[e].append((w, fname, a, kw, sem, n))
        self.c[sem] += n
        return (sem, self.c[sem])

    def wait(self, e, w):
        w = self._fix(x for x in w if x is not None)
        if w:
            self.q[e].append((w, None, (), {}, None, 0))

    def now(self, sem):
        return (sem, self.c[sem])

    def barrier(self):
        snap = [(s, self.c[s]) for s in ("sy", "gp", "pe", "act", "dv", "cc")]
        for e in ENGS:
            self.wait(e, [sv for sv in snap if sv[0] != e and sv[1] > 0])

    def realize_one(self, e, pr, sems):
        for (w, fname, a, kw, sem, n) in self.q[e]:
            for (s, v) in w:
                if v > 0:
                    pr.wait_ge(sems[s], v)
            if fname is not None:
                ins = getattr(pr, fname)(*a, **kw)
                if n > 0:
                    ins.then_inc(sems[sem], n)


def blockify(a):
    a = np.ascontiguousarray(a, dtype=np.float32)
    K, M = a.shape
    nch = K // 128
    return np.ascontiguousarray(
        a.reshape(nch, 128, M).transpose(1, 0, 2).reshape(128, nch * M))


def build_nc(stage=99, debug=False):
    nc = bass.Bass("TRN2", target_bir_lowering=False)
    DT = nc.dram_tensor

    x_in = DT("x_fm", [D, TOK], F32, kind="ExternalInput").ap()
    w_q0 = DT("w_q0", [128, NCHD * 512], F32, kind="ExternalInput").ap()
    w_kv0 = DT("w_kv0", [128, NCHD * 256], F32, kind="ExternalInput").ap()
    wp0_o = DT("wp0_o", [128, 4 * D], F32, kind="ExternalInput").ap()
    w0T = DT("w0T", [128, NCHD * D], F32, kind="ExternalInput").ap()
    b0_b = DT("b0_b", [128, 16], F32, kind="ExternalInput").ap()
    qg_rep = DT("qg_rep", [1, 512], F32, kind="ExternalInput").ap()
    w_q1n = DT("w_q1n", [128, NCHD * 512], F32, kind="ExternalInput").ap()
    w_q1p = DT("w_q1p", [128, NCHD * 256], F32, kind="ExternalInput").ap()
    w_kva = DT("w_kva", [128, NCHD * 576], F32, kind="ExternalInput").ap()
    w_kvb = DT("w_kvb", [128, 4 * 1024], F32, kind="ExternalInput").ap()
    kvns_in = DT("kvns_in", [128, 4], F32, kind="ExternalInput").ap()
    kvns_c = [DT(f"kvns_c{i}", [128, 1], F32, kind="ExternalInput").ap()
              for i in range(4)]
    wo1_o = DT("wo1_o", [128, 4 * D], F32, kind="ExternalInput").ap()
    w_a2Q = DT("w_a2Q", [128, NCHD * 512], F32, kind="ExternalInput").ap()
    w_a2K = DT("w_a2K", [128, NCHD * 512], F32, kind="ExternalInput").ap()
    w_a2V = DT("w_a2V", [128, NCHD * 512], F32, kind="ExternalInput").ap()
    wp2_o = DT("wp2_o", [128, 4 * D], F32, kind="ExternalInput").ap()
    sraw_in = DT("sraw_in", [128, 4], F32, kind="ExternalInput").ap()
    wrec8_in = DT("wrec8_in", [8, 4], F32, kind="ExternalInput").ap()
    cos0_t = DT("cos0_t", [8, T], F32, kind="ExternalInput").ap()
    sin0_t = DT("sin0_t", [8, T], F32, kind="ExternalInput").ap()
    cos1_t = DT("cos1_t", [32, T], F32, kind="ExternalInput").ap()
    sin1_t = DT("sin1_t", [32, T], F32, kind="ExternalInput").ap()
    cos1o_t = DT("cos1o_t", [32, TOK], F32, kind="ExternalInput").ap()
    sin1o_t = DT("sin1o_t", [32, TOK], F32, kind="ExternalInput").ap()
    masks_in = DT("masks_in", [128, 4 * 512], F32, kind="ExternalInput").ap()

    xout = DT("xout", [D, TOK], F32, kind="ExternalOutput").ap()

    xres = DT("xres", [D, TOK], F32).ap()
    agx_i = [DT(f"agx{l}_i", [D, TOK], F32) for l in range(2)]
    agx_o = [DT(f"agx{l}_o", [4 * D, TOK], F32) for l in range(2)]
    agkv_i = DT("agkv_i", [576, TOK], F32)
    agkv_o = DT("agkv_o", [4 * 576, TOK], F32)
    ystage = DT("ystage", [512, T], F32).ap()
    rs_i = [DT(f"rs{k}_i", [4 * D, TOK], F32) for k in range(3)]
    rs_o = [DT(f"rs{k}_o", [D, TOK], F32) for k in range(3)]
    qs0 = DT("qs0", [512, T], F32).ap()
    qs1n = DT("qs1n", [512, T], F32).ap()
    qs1p = DT("qs1p", [256, T], F32).ap()
    qs2 = DT("qs2", [512, T], F32).ap()
    ks2 = DT("ks2", [512, T], F32).ap()
    vs2 = DT("vs2", [T, 512], F32).ap()

    dbg = {}
    if debug:
        for nm, sh in [("xn0", [D, TOK]), ("q0", [512, T]), ("k0", [128, T]),
                       ("v0", [128, T]), ("y0", [512, T]), ("x1", [D, TOK]),
                       ("kv1", [576, TOK]), ("q1n", [512, T]),
                       ("q1p", [256, T]), ("y1", [512, T]), ("qf2", [512, T]),
                       ("kf2", [512, T]), ("v2d", [T, 512]), ("y2", [512, T]),
                       ("s0", [D, TOK]), ("scl", [128, 8]),
                       ("agkvo", [4 * 576, TOK])]:
            dbg[nm] = DT("dbg_" + nm, sh, F32, kind="ExternalOutput").ap()

    P = Prog()

    with ExitStack() as es:
        sb = lambda n_, sh: es.enter_context(nc.sbuf_tensor(n_, sh, F32))
        psa = lambda n_, sh: es.enter_context(nc.psum_tensor(n_, sh, F32))

        AR1 = sb("AR1", [128, 8192])
        AR2 = sb("AR2", [128, 8192])
        AR3 = sb("AR3", [128, 8192])
        AR4 = sb("AR4", [128, 9216])
        masks = sb("masks", [128, 4 * 512])
        cs0 = sb("cs0", [8, 512])
        sn0 = sb("sn0", [8, 512])
        qgq = sb("qgq", [1, 512])
        sqk2 = sb("sqk2", [128, 8])
        sr8 = sb("sr8", [8, 8])
        m8 = sb("m8", [8, 4])
        mc8 = sb("mc8", [8, 4])
        kvns = sb("kvns", [128, 4])
        kvnsc = [sb(f"kvnsc{i}", [128, 1]) for i in range(4)]
        scl = sb("scl", [1, 4])
        s128 = sb("s128", [128, 2])
        b0s = sb("b0s", [128, 16])
        wacc = sb("wacc", [128, 2])
        mshift = sb("mshift", [128, 4])
        q_sb = [sb(f"q_sb{i}", [128, 512]) for i in range(2)]
        qp_sb = [sb(f"qp_sb{i}", [64, 512]) for i in range(2)]
        e_sb = [sb(f"e_sb{i}", [128, 512]) for i in range(2)]
        racc = [sb(f"racc{i}", [128, 512]) for i in range(2)]
        sq_t = [sb(f"sq_t{i}", [128, 512]) for i in range(2)]
        tA = sb("tA", [128, 512])
        tB = sb("tB", [128, 512])
        xp = [sb(f"xp{i}", [128, 512]) for i in range(2)]
        t32 = [sb(f"t32_{i}", [32, 512]) for i in range(4)]
        row = [sb(f"row{i}", [1, 512]) for i in range(4)]
        ones_c = sb("ones_c", [128, 1])
        ones_r = sb("ones_r", [1, 128])

        pA = psa("pA", [128, 512])
        pB = psa("pB", [128, 512])
        pC = psa("pC", [128, 512])
        pD = psa("pD", [128, 512])
        pE = psa("pE", [128, 512])
        pF = psa("pF", [128, 512])
        pS1 = psa("pS1", [1, 512])
        pS2 = psa("pS2", [1, 512])

        sems = {s: es.enter_context(nc.semaphore(s)) for s in
                ("sy", "gp", "pe", "act", "dv", "cc")}

        for dst, src in [(masks[:], masks_in), (qgq[:], qg_rep),
                         (sqk2[:, 0:4], sraw_in), (sqk2[:, 4:8], sraw_in),
                         (sr8[:, 0:4], wrec8_in), (sr8[:, 4:8], wrec8_in),
                         (m8[:], wrec8_in), (kvns[:], kvns_in),
                         (b0s[:], b0_b)]:
            P.op("sy", "dma_start", out=dst, in_=src)
        for c in range(4):
            P.op("sy", "dma_start", out=kvnsc[c][:], in_=kvns_c[c])
        ld0 = P.now("sy")
        P.op("dv", "memset", ones_c[:], 1.0, w=[ld0])
        P.op("dv", "memset", ones_r[:], 1.0)
        P.op("dv", "memset", mshift[:, 0:1], -M0)
        P.op("dv", "memset", mshift[:, 1:2], -M1)
        P.op("dv", "memset", mshift[:, 2:3], -M2)
        P.op("dv", "tensor_scalar", qgq[:], qgq[:], SC0, None, ALU.mult)
        P.op("dv", "tensor_scalar_max", sqk2[:], sqk2[:], 1e-8)
        P.op("act", "activation", sqk2[:], sqk2[:], AF.Sqrt, w=[P.now("dv")])
        P.op("dv", "tensor_scalar_max", sr8[:], sr8[:], 1e-8)
        P.op("act", "activation", sr8[:], sr8[:], AF.Sqrt, w=[P.now("dv")])
        P.op("dv", "tensor_scalar", m8[:], m8[:], -0.1, None, ALU.add)
        P.op("act", "activation", m8[:], m8[:], AF.Sign, w=[P.now("dv")])
        P.op("dv", "tensor_scalar_max", m8[:], m8[:], 0.0, w=[P.now("act")])
        P.op("dv", "tensor_scalar", mc8[:], m8[:], -1.0, 1.0, ALU.mult,
             ALU.add)
        P.barrier()

        last_a = None
        for p in range(16):
            half = AR1[:, (p % 2) * 2048:(p % 2) * 2048 + 2048]
            wld = P.op("sy", "dma_start", out=half,
                       in_=w0T[:, p * 2048:(p + 1) * 2048],
                       w=[last_a] if p >= 2 else ())
            a = P.op("act", "activation", AR3[:, 0:2048], half, AF.Abs,
                     accum_out=wacc[:, 0:1], w=[wld, P.now("dv")])
            last_a = a
            if p == 0:
                P.op("dv", "tensor_copy", wacc[:, 1:2], wacc[:, 0:1], w=[a])
            else:
                P.op("dv", "tensor_add", wacc[:, 1:2], wacc[:, 1:2],
                     wacc[:, 0:1], w=[a])
        P.op("pe", "matmul", pS1[0:1, 0:1], wacc[:, 1:2], ones_c[:],
             start=True, stop=True, w=[P.now("dv")])
        P.op("dv", "tensor_scalar", scl[0:1, 0:1], pS1[0:1, 0:1],
             1.0 / (2048.0 * 2048.0), None, ALU.mult, w=[P.now("pe")])
        P.op("pe", "matmul", pF[:, 0:1], ones_r[:], scl[0:1, 0:1], start=True,
             stop=True, w=[P.now("dv")])
        P.op("dv", "tensor_copy", s128[:, 0:1], pF[:, 0:1], w=[P.now("pe")])
        P.op("dv", "memset", row[3][:], 1.0)
        P.op("pe", "matmul", pS2[0:1, :], scl[0:1, 0:1], row[3][:],
             start=True, stop=True, w=[P.now("dv")])
        P.op("dv", "tensor_scalar", row[2][:], pS2[0:1, :], 1e-10, 0.5,
             ALU.add, ALU.mult, w=[P.now("pe")])
        P.op("pe", "matmul", pF[:, 1:2], ones_r[:], row[2][0:1, 0:1],
             start=True, stop=True, w=[P.now("dv")])
        P.op("dv", "tensor_copy", s128[:, 1:2], pF[:, 1:2], w=[P.now("pe")])
        P.barrier()

        def stream_chunk(base_ap, rows0, cols0, buf, war=None):
            last = None
            for ch in range(NCHD):
                last = P.op("sy", "dma_start",
                            out=buf[:, ch * 512:(ch + 1) * 512],
                            in_=base_ap[rows0 + ch * 128:rows0 + (ch + 1) * 128,
                                        cols0:cols0 + 512],
                            w=[war] if ch == 0 else ())
            return last

        def agx_src(l, tc):
            jj = tc // 2
            return (agx_o[l].ap(), jj * D, (tc % 2) * 512)

        def ln_pass(xsrc, agx_dst):
            for tc in range(2):
                ld = stream_chunk(xsrc, 0, tc * 512, AR2, war=P.now("dv"))
                for ch in range(NCHD):
                    P.op("pe", "matmul", pS1[:], ones_c[:],
                         AR2[:, ch * 512:(ch + 1) * 512], start=(ch == 0),
                         stop=(ch == 15),
                         w=[ld, P.now("dv")] if ch == 0 else ())
                sqm = [None, None]
                for ch in range(NCHD):
                    s = P.op("act", "activation", sq_t[ch % 2][:],
                             AR2[:, ch * 512:(ch + 1) * 512], AF.Square,
                             w=[sqm[ch % 2],
                                P.now("pe") if ch < 2 else None])
                    sqm[ch % 2] = P.op("pe", "matmul", pS2[:], ones_c[:],
                                       sq_t[ch % 2][:], start=(ch == 0),
                                       stop=(ch == 15), w=[s])
                P.op("dv", "tensor_scalar", row[0][:], pS1[:], 1.0 / 2048.0,
                     None, ALU.mult, w=[P.now("pe")])
                P.op("dv", "tensor_scalar", row[1][:], pS2[:], 1.0 / 2048.0,
                     None, ALU.mult)
                P.op("dv", "tensor_tensor", row[2][:], row[0][:], row[0][:],
                     ALU.mult)
                P.op("dv", "tensor_sub", row[1][:], row[1][:], row[2][:])
                P.op("dv", "tensor_scalar", row[1][:], row[1][:], 1e-5, None,
                     ALU.add)
                P.op("dv", "reciprocal", row[1][:], row[1][:])
                P.op("act", "activation", row[1][:], row[1][:], AF.Sqrt,
                     w=[P.now("dv")])
                P.op("dv", "tensor_tensor", row[2][:], row[0][:], row[1][:],
                     ALU.mult, w=[P.now("act")])
                P.op("dv", "tensor_scalar", row[2][:], row[2][:], -1.0, None,
                     ALU.mult)
                P.op("pe", "matmul", pE[:], ones_r[:], row[1][0:1, :],
                     start=True, stop=True, w=[P.now("dv")])
                P.op("pe", "matmul", pF[:], ones_r[:], row[2][0:1, :],
                     start=True, stop=True)
                bc = P.now("pe")
                xd = [None, None]
                for ch in range(NCHD):
                    o = xp[ch % 2]
                    P.op("dv", "tensor_tensor", o[:],
                         AR2[:, ch * 512:(ch + 1) * 512], pE[:], ALU.mult,
                         w=[bc, xd[ch % 2]])
                    P.op("dv", "tensor_add", o[:], o[:], pF[:])
                    xd[ch % 2] = P.op(
                        "sy", "dma_start",
                        out=agx_dst.ap()[ch * 128:(ch + 1) * 128,
                                         tc * 512:(tc + 1) * 512],
                        in_=o[:], w=[P.now("dv")])
            P.barrier()

        def allgather(src, dst):
            return P.op("gp", "collective_compute", "AllGather", ALU.bypass,
                        replica_groups=GROUPS, ins=[src.ap().opt()],
                        outs=[dst.ap().opt()], sem="cc")

        def reduce_scatter(src, dst):
            return P.op("gp", "collective_compute", "ReduceScatter", ALU.add,
                        replica_groups=GROUPS, ins=[src.ap().opt()],
                        outs=[dst.ap().opt()], sem="cc")

        def rope_rms0(ph, cs, sn, gain_lhsT, out_buf, wout=()):
            P.op("act", "activation", sq_t[0][:], ph[:], AF.Square,
                 w=[P.now("pe")])
            P.op("pe", "matmul", pS1[:], ones_c[:], sq_t[0][:], start=True,
                 stop=True, w=[P.now("act")])
            P.op("dv", "tensor_scalar", row[0][:], pS1[:], 1.0 / 128.0,
                 EPS_RMS0, ALU.mult, ALU.add, w=[P.now("pe")])
            P.op("dv", "reciprocal", row[0][:], row[0][:])
            P.op("act", "activation", row[0][:], row[0][:], AF.Sqrt,
                 w=[P.now("dv")])
            P.op("pe", "matmul", pE[:], gain_lhsT, row[0][0:1, :], start=True,
                 stop=True, w=[P.now("act"), P.now("dv")])
            qf = sq_t[1]
            P.op("act", "copy", qf[:, :], ph[:, :], w=[P.now("dv")])
            P.op("dv", "tensor_copy", t32[0][0:8, :], qf[0:8, :],
                 w=[P.now("act")])
            P.op("dv", "tensor_copy", t32[1][0:8, :], qf[32:40, :])
            P.op("dv", "tensor_tensor", t32[2][0:8, :], t32[0][0:8, :], cs,
                 ALU.mult)
            P.op("dv", "tensor_tensor", t32[3][0:8, :], t32[1][0:8, :], sn,
                 ALU.mult)
            P.op("dv", "tensor_add", qf[0:8, :], t32[2][0:8, :],
                 t32[3][0:8, :])
            P.op("dv", "tensor_tensor", t32[2][0:8, :], t32[0][0:8, :], sn,
                 ALU.mult)
            P.op("dv", "tensor_tensor", t32[3][0:8, :], t32[1][0:8, :], cs,
                 ALU.mult)
            P.op("dv", "tensor_sub", qf[32:40, :], t32[2][0:8, :],
                 t32[3][0:8, :])
            ph_free = [P.now("dv"), P.now("act")]
            P.wait("dv", [P.now("act"), P.now("pe")] + list(wout))
            P.op("dv", "tensor_tensor", out_buf[:], qf[:], pE[:], ALU.mult)
            return ph_free, P.now("dv")

        def q0_pass():
            P.op("sy", "dma_start", out=AR4[:, 0:NCHD * 512], in_=w_q0)
            wld = P.now("sy")
            bufs = [AR1, AR2]
            cons = [None, None]
            hp = [pA, pB, pC, pD]
            bank_free = [[]] * 4
            out_free = [None, None]
            for tc in range(8):
                buf = bufs[tc % 2]
                ld = stream_chunk(*agx_src(0, tc), buf, war=cons[tc % 2])
                P.op("sy", "dma_start", out=cs0[:],
                     in_=cos0_t[:, tc * 512:(tc + 1) * 512], w=[P.now("dv")])
                tld = P.op("sy", "dma_start", out=sn0[:],
                           in_=sin0_t[:, tc * 512:(tc + 1) * 512])
                P.wait("dv", [tld])
                for h in range(4):
                    ph = hp[h]
                    for ch in range(NCHD):
                        P.op("pe", "matmul", ph[:],
                             AR4[:, ch * 512 + h * 128: ch * 512 + h * 128 + 128],
                             buf[:, ch * 512:(ch + 1) * 512],
                             start=(ch == 0), stop=(ch == 15),
                             w=[wld, ld] + bank_free[h] if ch == 0 else ())
                    out = q_sb[h % 2]
                    bank_free[h], od = rope_rms0(
                        ph, cs0[:], sn0[:],
                        qgq[0:1, h * 128:(h + 1) * 128], out,
                        wout=[out_free[h % 2]])
                    out_free[h % 2] = P.op(
                        "sy", "dma_start",
                        out=qs0[h * 128:(h + 1) * 128,
                                tc * 512:(tc + 1) * 512],
                        in_=out[:], w=[od])
                cons[tc % 2] = P.now("pe")
            P.barrier()

        def kv0_pass2():
            P.op("sy", "dma_start", out=AR4[:, 0:NCHD * 256], in_=w_kv0,
                 w=[P.now("pe")])
            wld = P.now("sy")
            bufs = [AR1, AR2]
            cons = [None, None]
            pA_free = []
            pB_free = None
            for tc in range(8):
                buf = bufs[tc % 2]
                ld = stream_chunk(*agx_src(0, tc), buf, war=cons[tc % 2])
                P.op("sy", "dma_start", out=cs0[:],
                     in_=cos0_t[:, tc * 512:(tc + 1) * 512], w=[P.now("dv")])
                tld = P.op("sy", "dma_start", out=sn0[:],
                           in_=sin0_t[:, tc * 512:(tc + 1) * 512])
                P.wait("dv", [tld])
                for ch in range(NCHD):
                    P.op("pe", "matmul", pA[:],
                         AR4[:, ch * 256: ch * 256 + 128],
                         buf[:, ch * 512:(ch + 1) * 512], start=(ch == 0),
                         stop=(ch == 15),
                         w=[wld, ld] + pA_free if ch == 0 else ())
                pA_free, _ = rope_rms0(
                    pA, cs0[:], sn0[:], ones_r[0:1, :],
                    AR3[:, tc * 512:(tc + 1) * 512])
                for sbk in range(4):
                    for ch in range(NCHD):
                        P.op("pe", "matmul", pB[:, sbk * 128:(sbk + 1) * 128],
                             buf[:, ch * 512 + sbk * 128:
                                 ch * 512 + sbk * 128 + 128],
                             AR4[:, ch * 256 + 128: ch * 256 + 256],
                             start=(ch == 0), stop=(ch == 15),
                             w=[pB_free] if (ch == 0 and sbk == 0) else ())
                P.op("act", "copy", AR3[:, T + tc * 512: T + (tc + 1) * 512],
                     pB[:], w=[P.now("pe")])
                pB_free = P.now("act")
                cons[tc % 2] = P.now("pe")
            P.barrier()

        def attention(layer, qsrc, qpsrc, mcol, kv_setup, escale=None):
            ekw = {} if escale is None else dict(scale=escale)
            for h in range(4):
                P.barrier()
                kready = kv_setup(h)
                qld_free = [None, None]
                e_cons = [[], []]
                sp_free = [None, None]
                pav_free = [None, None]
                racc_free = None
                pS1_free = None
                pF_free = None
                tA_free = None
                tB_free = None
                out_dma = [None, None]
                for qc in range(8):
                    nkt = 4 * qc + 4
                    qb = q_sb[qc % 2]
                    qld = P.op("sy", "dma_start", out=qb[:],
                               in_=qsrc[h * 128:(h + 1) * 128,
                                        qc * 512:(qc + 1) * 512],
                               w=[qld_free[qc % 2]])
                    if qpsrc is not None:
                        qpb = qp_sb[qc % 2]
                        qld = P.op("sy", "dma_start", out=qpb[:],
                                   in_=qpsrc[h * 64:(h + 1) * 64,
                                             qc * 512:(qc + 1) * 512])
                    pav = pC if qc % 2 == 0 else pD
                    pend_av = None
                    for kt in range(nkt):
                        sp = pA if kt % 2 == 0 else pB
                        diag = kt >= 4 * qc
                        wmm = [sp_free[kt % 2], kready]
                        if kt == 0:
                            wmm.append(qld)
                        if layer == 1:
                            P.op("pe", "matmul", sp[:],
                                 AR3[:, kt * 128:(kt + 1) * 128], qb[:],
                                 start=True, stop=False, w=wmm)
                            P.op("pe", "matmul", sp[:],
                                 AR4[0:64, kt * 128:(kt + 1) * 128], qpb[:],
                                 start=False, stop=True)
                        else:
                            P.op("pe", "matmul", sp[:],
                                 AR3[:, kt * 128:(kt + 1) * 128], qb[:],
                                 start=True, stop=True, w=wmm)
                        mmd = P.now("pe")
                        if pend_av is not None:
                            P.op("pe", "matmul", *pend_av[0], **pend_av[1])
                            e_cons[pend_av[2]].append(P.now("pe"))
                        if diag:
                            P.op("act", "activation", tA[:], sp[:], AF.Exp,
                                 bias=mshift[:, mcol:mcol + 1],
                                 w=[mmd, tA_free], **ekw)
                            sp_free[kt % 2] = P.now("act")
                            P.op("dv", "tensor_tensor", e_sb[kt % 2][:],
                                 tA[:],
                                 masks[:, (kt - 4 * qc) * 512:
                                       (kt - 4 * qc + 1) * 512],
                                 ALU.mult, w=[P.now("act")] + e_cons[kt % 2])
                            tA_free = P.now("dv")
                            e_ready = P.now("dv")
                        else:
                            P.op("act", "activation", e_sb[kt % 2][:], sp[:],
                                 AF.Exp, bias=mshift[:, mcol:mcol + 1],
                                 w=[mmd] + e_cons[kt % 2], **ekw)
                            sp_free[kt % 2] = P.now("act")
                            e_ready = P.now("act")
                        e_cons[kt % 2] = []
                        if kt == 0:
                            P.op("dv", "tensor_copy", racc[0][:], e_sb[0][:],
                                 w=[e_ready, racc_free])
                        else:
                            P.op("dv", "tensor_add", racc[kt % 2][:],
                                 racc[(kt + 1) % 2][:], e_sb[kt % 2][:],
                                 w=[e_ready])
                        e_cons[kt % 2].append(P.now("dv"))
                        vap = AR3[:, T + kt * 128: T + (kt + 1) * 128]
                        avw = [e_ready]
                        if kt == 0:
                            avw.append(pav_free[qc % 2])
                        pend_av = ((pav[:], vap, e_sb[kt % 2][:]),
                                   dict(start=(kt == 0), stop=(kt == nkt - 1),
                                        w=avw), kt % 2)
                    P.op("pe", "matmul", *pend_av[0], **pend_av[1])
                    e_cons[pend_av[2]].append(P.now("pe"))
                    av_done = P.now("pe")
                    P.op("pe", "matmul", pS1[:], ones_c[:],
                         racc[(nkt - 1) % 2][:], start=True, stop=True,
                         w=[P.now("dv"), pS1_free])
                    racc_free = P.now("pe")
                    P.op("dv", "reciprocal", row[0][:], pS1[:],
                         w=[P.now("pe")])
                    pS1_free = P.now("dv")
                    P.op("pe", "matmul", pF[:], ones_r[:], row[0][0:1, :],
                         start=True, stop=True, w=[P.now("dv"), pF_free])
                    P.op("dv", "tensor_copy", tB[:], pF[:],
                         w=[P.now("pe"), tB_free])
                    pF_free = P.now("dv")
                    ob = sq_t[qc % 2]
                    P.op("dv", "tensor_tensor", ob[:], pav[:], tB[:],
                         ALU.mult, w=[av_done, out_dma[qc % 2]])
                    tB_free = P.now("dv")
                    pav_free[qc % 2] = P.now("dv")
                    out_dma[qc % 2] = P.op(
                        "sy", "dma_start",
                        out=ystage[h * 128:(h + 1) * 128,
                                   qc * 512:(qc + 1) * 512],
                        in_=ob[:], w=[P.now("dv")])
                    qld_free[qc % 2] = P.now("pe")
            P.barrier()

        def post_pass(w_ap, rs_in):
            P.op("sy", "dma_start", out=AR4[:, 0:4 * D], in_=w_ap)
            wld = P.now("sy")
            cons = None
            evf = [None, None]
            for tc in range(8):
                lds = []
                for hh in range(4):
                    lds.append(P.op(
                        "sy", "dma_start",
                        out=AR2[:, hh * 512:(hh + 1) * 512],
                        in_=ystage[hh * 128:(hh + 1) * 128,
                                   tc * 512:(tc + 1) * 512],
                        w=[cons] if hh == 0 else ()))
                ld = lds[-1]
                for mc in range(16):
                    pg = pA if mc % 2 == 0 else pB
                    for hh in range(4):
                        P.op("pe", "matmul", pg[:],
                             AR4[:, hh * D + mc * 128: hh * D + mc * 128 + 128],
                             AR2[:, hh * 512:(hh + 1) * 512],
                             start=(hh == 0), stop=(hh == 3),
                             w=[wld, ld, evf[mc % 2]] if hh == 0 else ())
                    o = xp[mc % 2]
                    P.op("act", "copy", o[:], pg[:],
                         w=[P.now("pe"), evf[mc % 2]])
                    evf[mc % 2] = P.op(
                        "sy", "dma_start",
                        out=rs_in.ap()[(tc // 2) * D + mc * 128:
                                       (tc // 2) * D + (mc + 1) * 128,
                                       (tc % 2) * 512:(tc % 2) * 512 + 512],
                        in_=o[:], w=[P.now("act")])
                cons = P.now("pe")
            P.barrier()

        ln_pass(x_in, agx_i[0])
        P.wait("gp", [P.now("sy")])
        allgather(agx_i[0], agx_o[0])
        if stage >= 2:
            P.wait("sy", [P.now("cc")])
            q0_pass()
            kv0_pass2()
        if stage >= 3:
            attention(0, qs0, None, 0, lambda h: None)
        if stage >= 4:
            post_pass(wp0_o, rs_i[0])
            P.wait("gp", [P.now("sy")])
            reduce_scatter(rs_i[0], rs_o[0])

            P.wait("sy", [P.now("cc")])
            for tc in range(2):
                gld = stream_chunk(agx_i[0].ap(), 0, tc * 512, AR2,
                                   war=P.now("pe"))
                wfree = [None, None]
                sfree = [None, None]
                xfree = [None, None]
                ofree = [None, None]
                gsil = [None, None]
                for p in range(8):
                    half = AR1[:, (p % 2) * 4096:(p % 2) * 4096 + 4096]
                    lds = []
                    for ch in range(NCHD):
                        lds.append(P.op(
                            "sy", "dma_start",
                            out=half[:, ch * 256:(ch + 1) * 256],
                            in_=w0T[:, ch * 2048 + p * 256:
                                    ch * 2048 + (p + 1) * 256],
                            w=[wfree[p % 2]] if ch == 0 else ()))
                    wld = lds[-1]
                    u1 = AR3[:, 0:4096]
                    u2 = AR3[:, 4096:8192]
                    P.op("act", "activation", u1, half, AF.Abs,
                         w=[wld, P.now("dv"), P.now("pe")])
                    P.op("act", "activation", u2, half, AF.Sign)
                    P.op("dv", "tensor_scalar", u1, u1, s128[:, 1:2], None,
                         ALU.subtract, w=[P.now("act")])
                    P.op("dv", "tensor_scalar", u1, u1, 1e30, 1.0, ALU.mult,
                         ALU.min)
                    P.op("dv", "tensor_scalar_max", u1, u1, 0.0)
                    P.op("dv", "tensor_tensor", u1, u1, u2, ALU.mult)
                    wtr = P.now("dv")
                    wfree[p % 2] = wtr
                    for mc in range(2):
                        oc = p * 2 + mc
                        pg = pE if mc == 0 else pF
                        for ch in range(NCHD):
                            P.op("pe", "matmul", pg[:],
                                 AR3[:, ch * 256 + mc * 128:
                                     ch * 256 + mc * 128 + 128],
                                 AR2[:, ch * 512:(ch + 1) * 512],
                                 start=(ch == 0), stop=(ch == 15),
                                 w=[wtr, gld, gsil[mc]] if ch == 0 else ())
                        gt = sq_t[mc]
                        P.op("act", "activation", gt[:], pg[:], AF.Silu,
                             bias=b0s[:, oc:oc + 1], scale=s128[:, 0:1],
                             w=[P.now("pe"), sfree[mc]])
                        gsil[mc] = P.now("act")
                        st = q_sb[mc]
                        sld = P.op("sy", "dma_start", out=st[:],
                                   in_=rs_o[0].ap()[oc * 128:(oc + 1) * 128,
                                                    tc * 512:(tc + 1) * 512],
                                   w=[sfree[mc]])
                        xb = xp[mc]
                        xld = P.op("sy", "dma_start", out=xb[:],
                                   in_=x_in[oc * 128:(oc + 1) * 128,
                                            tc * 512:(tc + 1) * 512],
                                   w=[xfree[mc]])
                        ot = tA if mc == 0 else tB
                        P.op("dv", "tensor_tensor", ot[:], st[:], gt[:],
                             ALU.mult, w=[P.now("act"), sld, ofree[mc]])
                        P.op("dv", "tensor_add", ot[:], ot[:], xb[:],
                             w=[xld])
                        sfree[mc] = P.now("dv")
                        xfree[mc] = sfree[mc]
                        ofree[mc] = P.op(
                            "sy", "dma_start",
                            out=xres[oc * 128:(oc + 1) * 128,
                                     tc * 512:(tc + 1) * 512],
                            in_=ot[:], w=[P.now("dv")])
            P.barrier()

        if stage >= 5:
            ln_pass(xres, agx_i[1])
            P.wait("gp", [P.now("sy")])
            allgather(agx_i[1], agx_o[1])

            P.op("sy", "dma_start", out=AR4[:, 0:NCHD * 576], in_=w_kva)
            P.op("sy", "dma_start", out=AR1[0:32, 2048:3072], in_=cos1o_t)
            P.op("sy", "dma_start", out=AR1[0:32, 3072:4096], in_=sin1o_t)
            wld = P.now("sy")
            kvd = [None, None]
            kped = None
            for tc in range(2):
                ld = stream_chunk(agx_i[1].ap(), 0, tc * 512, AR2,
                                  war=P.now("pe"))
                for c in range(4):
                    for ch in range(NCHD):
                        P.op("pe", "matmul", pA[:],
                             AR4[:, ch * 576 + c * 128:
                                 ch * 576 + (c + 1) * 128],
                             AR2[:, ch * 512:(ch + 1) * 512],
                             start=(ch == 0), stop=(ch == 15),
                             w=[wld, ld, P.now("act")] if ch == 0 else ())
                    P.op("act", "copy", AR1[:, c * 512:(c + 1) * 512], pA[:],
                         w=[P.now("pe"), P.now("dv")])
                    if debug:
                        P.op("sy", "dma_start",
                             out=dbg["q1n"][c * 128:(c + 1) * 128,
                                            tc * 512:(tc + 1) * 512],
                             in_=AR1[:, c * 512:(c + 1) * 512],
                             w=[P.now("act")])
                for ch in range(NCHD):
                    P.op("pe", "matmul", pB[0:64, :],
                         AR4[:, ch * 576 + 512: ch * 576 + 576],
                         AR2[:, ch * 512:(ch + 1) * 512],
                         start=(ch == 0), stop=(ch == 15),
                         w=[P.now("act"), P.now("dv")] if ch == 0 else ())
                sqm = None
                for c in range(4):
                    P.op("act", "activation", sq_t[0][:],
                         AR1[:, c * 512:(c + 1) * 512], AF.Square,
                         w=[sqm, P.now("pe") if c == 0 else None])
                    sqm = P.op("pe", "matmul", pS1[:], ones_c[:], sq_t[0][:],
                               start=(c == 0), stop=(c == 3),
                               w=[P.now("act"), P.now("dv")] if c == 0
                               else [P.now("act")])
                P.op("dv", "tensor_scalar", row[0][:], pS1[:], 1.0 / 512.0,
                     1e-6, ALU.mult, ALU.add, w=[P.now("pe")])
                P.op("dv", "reciprocal", row[0][:], row[0][:])
                P.op("act", "activation", row[0][:], row[0][:], AF.Sqrt,
                     w=[P.now("dv")])
                P.op("pe", "matmul", pF[:], ones_r[:], row[0][0:1, :],
                     start=True, stop=True, w=[P.now("act"), P.now("dv")])
                P.op("dv", "tensor_copy", tB[:], pF[:], w=[P.now("pe")])
                bc = P.now("dv")
                if debug:
                    P.op("sy", "dma_start",
                         out=dbg["y1"][0:128, tc * 512:(tc + 1) * 512],
                         in_=tB[:], w=[bc])
                for c in range(4):
                    o = sq_t[c % 2]
                    o2 = e_sb[c % 2]
                    if debug:
                        P.op("sy", "dma_start",
                             out=dbg["qf2"][c * 128:(c + 1) * 128,
                                            tc * 512:(tc + 1) * 512],
                             in_=AR1[:, c * 512:(c + 1) * 512], w=[bc])
                    P.op("dv", "tensor_tensor", o[:],
                         AR1[:, c * 512:(c + 1) * 512], tB[:], ALU.mult,
                         w=[bc, kvd[c % 2]])
                    P.op("dv", "tensor_scalar", o2[:], o[:], kvnsc[c][:],
                         None, ALU.mult)
                    if debug:
                        P.op("sy", "dma_start",
                             out=dbg["kf2"][c * 128:(c + 1) * 128,
                                            tc * 512:(tc + 1) * 512],
                             in_=o[:], w=[P.now("dv")])
                    kvd[c % 2] = P.op(
                        "sy", "dma_start",
                        out=agkv_i.ap()[c * 128:(c + 1) * 128,
                                        tc * 512:(tc + 1) * 512],
                        in_=o[:], w=[P.now("dv")])
                P.op("act", "copy", tA[0:64, :], pB[0:64, :],
                     w=[P.now("pe"), P.now("dv")])
                P.op("dv", "tensor_copy", t32[0][:], tA[0:32, :],
                     w=[P.now("act")])
                P.op("dv", "tensor_copy", t32[1][:], tA[32:64, :])
                cs = AR1[0:32, 2048 + tc * 512: 2048 + (tc + 1) * 512]
                sn = AR1[0:32, 3072 + tc * 512: 3072 + (tc + 1) * 512]
                P.op("dv", "tensor_tensor", t32[2][:], t32[0][:], cs, ALU.mult,
                     w=[kped])
                P.op("dv", "tensor_tensor", t32[3][:], t32[1][:], sn, ALU.mult)
                P.op("dv", "tensor_sub", t32[2][:], t32[2][:], t32[3][:])
                d1 = P.op("sy", "dma_start",
                          out=agkv_i.ap()[512:544, tc * 512:(tc + 1) * 512],
                          in_=t32[2][:], w=[P.now("dv")])
                P.op("dv", "tensor_tensor", t32[2][:], t32[1][:], cs, ALU.mult,
                     w=[d1])
                P.op("dv", "tensor_tensor", t32[3][:], t32[0][:], sn, ALU.mult)
                P.op("dv", "tensor_add", t32[2][:], t32[2][:], t32[3][:])
                kped = P.op("sy", "dma_start",
                            out=agkv_i.ap()[544:576, tc * 512:(tc + 1) * 512],
                            in_=t32[2][:], w=[P.now("dv")])
            P.wait("gp", [P.now("sy")])
            if debug:
                P.op("gp", "dma_start", out=dbg["kv1"], in_=agkv_i.ap())
            allgather(agkv_i, agkv_o)
            P.barrier()

        if stage >= 6:
            P.op("sy", "dma_start", out=AR4[:, 0:NCHD * 512], in_=w_q1n)
            wld = P.now("sy")
            bufs = [AR1, AR2]
            cons = [None, None]
            hp = [pA, pB, pC, pD]
            bank_free = [None] * 4
            out_free = [None, None]
            for tc in range(8):
                buf = bufs[tc % 2]
                ld = stream_chunk(*agx_src(1, tc), buf, war=cons[tc % 2])
                for h in range(4):
                    ph = hp[h]
                    for ch in range(NCHD):
                        P.op("pe", "matmul", ph[:],
                             AR4[:, ch * 512 + h * 128:
                                 ch * 512 + h * 128 + 128],
                             buf[:, ch * 512:(ch + 1) * 512],
                             start=(ch == 0), stop=(ch == 15),
                             w=[wld, ld, bank_free[h]] if ch == 0 else ())
                    out = q_sb[h % 2]
                    P.op("dv", "tensor_scalar", out[:], ph[:], SC1, None,
                         ALU.mult, w=[P.now("pe"), out_free[h % 2]])
                    bank_free[h] = P.now("dv")
                    out_free[h % 2] = P.op(
                        "sy", "dma_start",
                        out=qs1n[h * 128:(h + 1) * 128,
                                 tc * 512:(tc + 1) * 512],
                        in_=out[:], w=[P.now("dv")])
                cons[tc % 2] = P.now("pe")
            P.barrier()
            P.op("sy", "dma_start", out=AR4[:, 0:NCHD * 256], in_=w_q1p)
            P.op("sy", "dma_start", out=AR3[0:32, 0:4096], in_=cos1_t,
                 w=[P.now("pe"), P.now("dv")])
            P.op("sy", "dma_start", out=AR3[0:32, 4096:8192], in_=sin1_t)
            wld = P.now("sy")
            cons = [None, None]
            pA_free = None
            out_free = [None, None]
            for tc in range(8):
                buf = bufs[tc % 2]
                ld = stream_chunk(*agx_src(1, tc), buf, war=cons[tc % 2])
                cs = AR3[0:32, tc * 512:(tc + 1) * 512]
                sn = AR3[0:32, 4096 + tc * 512: 4096 + (tc + 1) * 512]
                for h in range(4):
                    for ch in range(NCHD):
                        P.op("pe", "matmul", pA[0:64, :],
                             AR4[:, ch * 256 + h * 64:
                                 ch * 256 + (h + 1) * 64],
                             buf[:, ch * 512:(ch + 1) * 512],
                             start=(ch == 0), stop=(ch == 15),
                             w=[wld, ld, pA_free] if ch == 0 else ())
                    out = qp_sb[h % 2]
                    P.op("act", "copy", tA[0:64, :], pA[0:64, :],
                         w=[P.now("pe"), P.now("dv")])
                    P.op("dv", "tensor_copy", t32[0][:], tA[0:32, :],
                         w=[P.now("act")])
                    P.op("dv", "tensor_copy", t32[1][:], tA[32:64, :])
                    pA_free = P.now("act")
                    P.op("dv", "tensor_tensor", t32[2][:], t32[0][:], cs,
                         ALU.mult)
                    P.op("dv", "tensor_tensor", t32[3][:], t32[1][:], sn,
                         ALU.mult)
                    P.op("dv", "tensor_sub", t32[2][:], t32[2][:], t32[3][:])
                    P.op("dv", "tensor_scalar", out[0:32, :], t32[2][:], SC1,
                         None, ALU.mult, w=[out_free[h % 2]])
                    P.op("dv", "tensor_tensor", t32[2][:], t32[1][:], cs,
                         ALU.mult)
                    P.op("dv", "tensor_tensor", t32[3][:], t32[0][:], sn,
                         ALU.mult)
                    P.op("dv", "tensor_add", t32[2][:], t32[2][:], t32[3][:])
                    P.op("dv", "tensor_scalar", out[32:64, :], t32[2][:], SC1,
                         None, ALU.mult)
                    out_free[h % 2] = P.op(
                        "sy", "dma_start",
                        out=qs1p[h * 64:(h + 1) * 64,
                                 tc * 512:(tc + 1) * 512],
                        in_=out[:], w=[P.now("dv")])
                cons[tc % 2] = P.now("pe")
            P.barrier()

        if stage >= 7:
            for c in range(4):
                arena = AR1 if c < 2 else AR2
                off = (c % 2) * 4096
                for jj in range(4):
                    P.op("sy", "dma_start",
                         out=arena[:, off + jj * 1024: off + (jj + 1) * 1024],
                         in_=agkv_o.ap()[jj * 576 + c * 128:
                                         jj * 576 + (c + 1) * 128, :])
            for jj in range(4):
                P.op("sy", "dma_start",
                     out=AR4[0:64, jj * 1024:(jj + 1) * 1024],
                     in_=agkv_o.ap()[jj * 576 + 512: jj * 576 + 576, :])
            P.op("sy", "dma_start", out=AR4[:, 4096:8192], in_=w_kvb)
            ckv_ld = P.now("sy")

            def ckv_ap(c, cols0, ncols):
                arena = AR1 if c < 2 else AR2
                off = (c % 2) * 4096
                return arena[:, off + cols0: off + cols0 + ncols]

            def kv1_setup(h):
                P.wait("pe", [ckv_ld])
                pEf = None
                pFf = None
                for tcc in range(8):
                    for c in range(4):
                        P.op("pe", "matmul", pE[:],
                             AR4[:, 4096 + c * 1024 + h * 128:
                                 4096 + c * 1024 + (h + 1) * 128],
                             ckv_ap(c, tcc * 512, 512),
                             start=(c == 0), stop=(c == 3),
                             w=[pEf] if c == 0 else ())
                    P.op("act", "copy", AR3[:, tcc * 512:(tcc + 1) * 512],
                         pE[:], w=[P.now("pe")])
                    pEf = P.now("act")
                for grp in range(8):
                    for sub in range(4):
                        kt = grp * 4 + sub
                        for c in range(4):
                            P.op("pe", "matmul",
                                 pF[:, sub * 128:(sub + 1) * 128],
                                 ckv_ap(c, kt * 128, 128),
                                 AR4[:, 4096 + c * 1024 + 512 + h * 128:
                                     4096 + c * 1024 + 512 + (h + 1) * 128],
                                 start=(c == 0), stop=(c == 3),
                                 w=[pFf] if (c == 0 and sub == 0) else ())
                    P.op("act", "copy",
                         AR3[:, T + grp * 512: T + (grp + 1) * 512], pF[:],
                         w=[P.now("pe")])
                    pFf = P.now("act")
                return P.now("act")

            attention(1, qs1n, qs1p, 1, kv1_setup)
            post_pass(wo1_o, rs_i[1])
            P.wait("gp", [P.now("sy")])
            reduce_scatter(rs_i[1], rs_o[1])

        if stage >= 8:
            def qk2_pass(w_ap, qcol, dst, is_q):
                P.op("sy", "dma_start", out=AR4[:, 0:NCHD * 512], in_=w_ap,
                     w=[P.now("pe")])
                wld = P.now("sy")
                bufs = [AR1, AR2]
                cons = [None, None]
                pA_free = None
                out_free = [None, None]
                for tc in range(8):
                    buf = bufs[tc % 2]
                    ld = stream_chunk(*agx_src(1, tc), buf, war=cons[tc % 2])
                    for h in range(4):
                        for ch in range(NCHD):
                            P.op("pe", "matmul", pA[:],
                                 AR4[:, ch * 512 + h * 128:
                                     ch * 512 + h * 128 + 128],
                                 buf[:, ch * 512:(ch + 1) * 512],
                                 start=(ch == 0), stop=(ch == 15),
                                 w=[wld, ld, pA_free] if ch == 0 else ())
                        out = q_sb[h % 2]
                        scol = (qcol + h)
                        P.op("dv", "tensor_scalar", out[:, :], pA[:, :],
                             sqk2[:, scol:scol + 1], None, ALU.mult,
                             w=[P.now("pe"), out_free[h % 2]])
                        pA_free = P.now("dv")
                        out_free[h % 2] = P.op(
                            "sy", "dma_start",
                            out=dst[h * 128:(h + 1) * 128,
                                    tc * 512:(tc + 1) * 512],
                            in_=out[:], w=[P.now("dv")])
                    cons[tc % 2] = P.now("pe")
                P.barrier()

            qk2_pass(w_a2Q, 0, qs2, True)
            qk2_pass(w_a2K, 4, ks2, False)

            def fixup_pass():
                for tc in range(8):
                    for h in range(4):
                        lq = P.op("sy", "dma_start", out=t32[0][0:8, :],
                                  in_=qs2[h * 128 + 96: h * 128 + 104,
                                          tc * 512:(tc + 1) * 512],
                                  w=[P.now("dv")])
                        lk = P.op("sy", "dma_start", out=t32[1][0:8, :],
                                  in_=ks2[h * 128 + 96: h * 128 + 104,
                                          tc * 512:(tc + 1) * 512])
                        P.op("dv", "tensor_scalar", t32[2][0:8, :],
                             t32[0][0:8, :], mc8[:, h:h + 1], None, ALU.mult,
                             w=[lq, lk])
                        P.op("dv", "tensor_scalar", tB[0:8, :],
                             t32[1][0:8, :], m8[:, h:h + 1], None, ALU.mult)
                        P.op("dv", "tensor_add", t32[2][0:8, :],
                             t32[2][0:8, :], tB[0:8, :])
                        P.op("dv", "tensor_scalar", t32[2][0:8, :],
                             t32[2][0:8, :], sr8[:, h:h + 1], None, ALU.mult)
                        d1 = P.op("sy", "dma_start",
                                  out=qs2[h * 128 + 96: h * 128 + 104,
                                          tc * 512:(tc + 1) * 512],
                                  in_=t32[2][0:8, :], w=[P.now("dv")])
                        P.op("dv", "tensor_scalar", t32[3][0:8, :],
                             t32[0][0:8, :], m8[:, h:h + 1], None, ALU.mult)
                        P.op("dv", "tensor_scalar", tB[0:8, :],
                             t32[1][0:8, :], mc8[:, h:h + 1], None, ALU.mult)
                        P.op("dv", "tensor_add", t32[3][0:8, :],
                             t32[3][0:8, :], tB[0:8, :])
                        P.op("dv", "tensor_scalar", t32[3][0:8, :],
                             t32[3][0:8, :], sr8[:, 4 + h:5 + h], None,
                             ALU.mult)
                        P.op("sy", "dma_start",
                             out=ks2[h * 128 + 96: h * 128 + 104,
                                     tc * 512:(tc + 1) * 512],
                             in_=t32[3][0:8, :], w=[P.now("dv"), d1])
                P.barrier()

            fixup_pass()

            P.op("sy", "dma_start", out=AR4[:, 0:NCHD * 512], in_=w_a2V)
            wld = P.now("sy")
            bufs = [AR1, AR2]
            cons = [None, None]
            pCf = [None, None]
            for tc in range(8):
                buf = bufs[tc % 2]
                ld = stream_chunk(*agx_src(1, tc), buf, war=cons[tc % 2])
                for sbk in range(4):
                    pg = pC if sbk % 2 == 0 else pD
                    for ch in range(NCHD):
                        P.op("pe", "matmul", pg[:],
                             buf[:, ch * 512 + sbk * 128:
                                 ch * 512 + sbk * 128 + 128],
                             AR4[:, ch * 512:(ch + 1) * 512],
                             start=(ch == 0), stop=(ch == 15),
                             w=[wld, ld, pCf[sbk % 2]] if ch == 0 else ())
                    o = xp[sbk % 2]
                    P.op("act", "copy", o[:], pg[:],
                         w=[P.now("pe"), pCf[sbk % 2]])
                    pCf[sbk % 2] = P.op(
                        "sy", "dma_start",
                        out=vs2[tc * 512 + sbk * 128:
                                tc * 512 + (sbk + 1) * 128, :],
                        in_=o[:], w=[P.now("act")])
                cons[tc % 2] = P.now("pe")
            P.barrier()

        if stage >= 9:
            def kv2_setup(h):
                k1 = P.op("sy", "dma_start", out=AR3[:, 0:T],
                          in_=ks2[h * 128:(h + 1) * 128, :])
                for kt in range(32):
                    k1 = P.op("sy", "dma_start",
                              out=AR3[:, T + kt * 128: T + (kt + 1) * 128],
                              in_=vs2[kt * 128:(kt + 1) * 128,
                                      h * 128:(h + 1) * 128])
                return k1

            attention(2, qs2, None, 2, kv2_setup, escale=SC2)
            post_pass(wp2_o, rs_i[2])
            P.wait("gp", [P.now("sy")])
            reduce_scatter(rs_i[2], rs_o[2])

        if stage >= 10:
            P.wait("sy", [P.now("cc")])
            ofree = [None, None]
            for tc in range(2):
                for oc in range(16):
                    st = q_sb[oc % 2]
                    gt = sq_t[oc % 2]
                    xb = xp[oc % 2]
                    w0 = [ofree[oc % 2]]
                    sld = P.op("sy", "dma_start", out=st[:],
                               in_=rs_o[1].ap()[oc * 128:(oc + 1) * 128,
                                                tc * 512:(tc + 1) * 512],
                               w=w0)
                    gld = P.op("sy", "dma_start", out=gt[:],
                               in_=rs_o[2].ap()[oc * 128:(oc + 1) * 128,
                                                tc * 512:(tc + 1) * 512])
                    xld = P.op("sy", "dma_start", out=xb[:],
                               in_=xres[oc * 128:(oc + 1) * 128,
                                        tc * 512:(tc + 1) * 512])
                    ge = e_sb[oc % 2]
                    P.op("act", "activation", ge[:], gt[:], AF.Silu,
                         w=[gld])
                    ot = tA if oc % 2 == 0 else tB
                    P.op("dv", "tensor_tensor", ot[:], st[:], ge[:], ALU.mult,
                         w=[P.now("act"), sld, ofree[oc % 2]])
                    P.op("dv", "tensor_add", ot[:], ot[:], xb[:], w=[xld])
                    ofree[oc % 2] = P.op(
                        "sy", "dma_start",
                        out=xout[oc * 128:(oc + 1) * 128,
                                 tc * 512:(tc + 1) * 512],
                        in_=ot[:], w=[P.now("dv")])
            P.barrier()

        if debug:
            P.wait("gp", [P.now("sy"), P.now("dv"), P.now("act"),
                          P.now("pe"), P.now("cc")])
            dumps = []
            P.op("gp", "dma_start", out=dbg["scl"][:, 0:2], in_=s128[:])
            P.op("gp", "dma_start", out=dbg["scl"][0:1, 4:8], in_=scl[:])
            P.op("gp", "dma_start", out=dbg["scl"][0:1, 2:4],
                 in_=wacc[0:1, :])
            if stage >= 1:
                dumps.append(("xn0", agx_i[0].ap()))
            if stage >= 2:
                dumps += [("q0", qs0), ("k0", AR3[:, 0:T]),
                          ("v0", AR3[:, T:2 * T])]
            if stage >= 3:
                dumps.append(("y0", ystage))
            if stage >= 4:
                dumps += [("x1", xres), ("s0", rs_o[0].ap())]
            if stage >= 5:
                dumps.append(("agkvo", agkv_o.ap()))
            if stage >= 6:
                dumps += [("q1n", qs1n), ("q1p", qs1p)]
            if stage == 7:
                dumps.append(("y1", ystage))
            if stage >= 8:
                dumps += [("qf2", qs2), ("kf2", ks2), ("v2d", vs2)]
            if stage == 9:
                dumps.append(("y2", ystage))
            if stage >= 5:
                dumps = [d for d in dumps if d[0] not in ("k0", "v0")]
            for nm, src in dumps:
                P.op("gp", "dma_start", out=dbg[nm], in_=src)

        P.barrier()
        P.wait("gp", [P.now("sy"), P.now("gp"), P.now("cc")])

        block = es.enter_context(nc.Block())

        @block.sync
        def _(sync):
            P.realize_one("sy", sync, sems)

        @block.gpsimd
        def _(gpsimd):
            P.realize_one("gp", gpsimd, sems)

        @block.tensor
        def _(tensor):
            P.realize_one("pe", tensor, sems)

        @block.scalar
        def _(scalar):
            P.realize_one("act", scalar, sems)

        @block.vector
        def _(vector):
            P.realize_one("dv", vector, sems)

    return nc


def make_inputs(inp, c):
    b, j = c // 4, c % 4
    heads = list(range(4 * j, 4 * j + 4))
    d = {}
    x = np.asarray(inp["L0_attn_x"], dtype=np.float32)
    d["x_fm"] = np.ascontiguousarray(x[b, j * TOK:(j + 1) * TOK, :].T)

    PERM0 = np.concatenate([np.arange(0, 8), np.arange(16, 40),
                            np.arange(8, 16), np.arange(40, 128)])
    wq = np.asarray(inp["L0_attn_wq"], np.float32)
    wqT = wq.T
    d["w_q0"] = blockify(np.concatenate(
        [wqT[:, h * 128:(h + 1) * 128][:, PERM0] for h in heads], axis=1))
    wkT = np.asarray(inp["L0_attn_wk"], np.float32).T
    wvT = np.asarray(inp["L0_attn_wv"], np.float32).T
    d["w_kv0"] = blockify(np.concatenate(
        [wkT[:, j * 128:(j + 1) * 128][:, PERM0],
         wvT[:, j * 128:(j + 1) * 128]], axis=1))
    wpT = np.asarray(inp["L0_attn_wp"], np.float32).T
    d["wp0_o"] = blockify(np.concatenate(
        [wpT[h * 128:(h + 1) * 128, :].T.T for h in heads], axis=0)
        .reshape(512, 2048))
    d["wp0_o"] = blockify(np.concatenate(
        [wpT[h * 128:(h + 1) * 128, :] for h in heads], axis=0))
    d["w0T"] = blockify(np.asarray(inp["L0_ffn_weight"], np.float32).T)
    d["b0_b"] = np.ascontiguousarray(
        np.asarray(inp["L0_ffn_bias"], np.float32).reshape(16, 128).T)
    qg = np.asarray(inp["L0_attn_qg"], np.float32)
    d["qg_rep"] = np.ascontiguousarray(
        np.repeat(qg[heads], 128)[None, :])

    wq1 = np.asarray(inp["L1_attn_wq"], np.float32)
    d["w_q1n"] = blockify(np.concatenate(
        [wq1[:, h * 192: h * 192 + 128] for h in heads], axis=1))
    d["w_q1p"] = blockify(np.concatenate(
        [wq1[:, h * 192 + 128: h * 192 + 192] for h in heads], axis=1))
    d["w_kva"] = blockify(np.asarray(inp["L1_attn_wkv_a"], np.float32))
    wkvb = np.asarray(inp["L1_attn_wkv_b"], np.float32)
    d["w_kvb"] = blockify(np.concatenate(
        [wkvb[:, h * 256: h * 256 + 128] for h in heads] +
        [wkvb[:, h * 256 + 128: h * 256 + 256] for h in heads], axis=1))
    kvns = np.asarray(inp["L1_attn_kv_norm_scale"], np.float32)
    d["kvns_in"] = np.ascontiguousarray(kvns.reshape(4, 128).T)
    for i in range(4):
        d[f"kvns_c{i}"] = np.ascontiguousarray(
            kvns[i * 128:(i + 1) * 128].reshape(128, 1))
    wo = np.asarray(inp["L1_attn_wo"], np.float32)
    d["wo1_o"] = blockify(np.concatenate(
        [wo[h * 128:(h + 1) * 128, :] for h in heads], axis=0))

    Wa = np.asarray(inp["L1_ffn_Wa"], np.float32)
    NH, Ds, R = 16, 120, 8
    q_std_end = NH * Ds
    k_std_end = 2 * NH * Ds
    v_end = k_std_end + 2048
    q_low_end = v_end + NH * R
    d["w_a2Q"] = blockify(np.concatenate(
        [np.concatenate([Wa[:, h * Ds: h * Ds + 96],
                         Wa[:, v_end + h * R: v_end + (h + 1) * R],
                         Wa[:, h * Ds + 96:(h + 1) * Ds]], axis=1)
         for h in heads], axis=1))
    d["w_a2K"] = blockify(np.concatenate(
        [np.concatenate([Wa[:, q_std_end + h * Ds: q_std_end + h * Ds + 96],
                         Wa[:, q_low_end + h * R: q_low_end + (h + 1) * R],
                         Wa[:, q_std_end + h * Ds + 96:
                            q_std_end + (h + 1) * Ds]],
                        axis=1) for h in heads], axis=1))
    d["w_a2V"] = blockify(np.concatenate(
        [Wa[:, k_std_end + h * 128: k_std_end + (h + 1) * 128]
         for h in heads], axis=1))
    Wp = np.asarray(inp["L1_ffn_Wp"], np.float32)
    d["wp2_o"] = blockify(np.concatenate(
        [Wp[h * 128:(h + 1) * 128, :] for h in heads], axis=0))
    w_std = np.asarray(inp["L1_ffn_w_std"], np.float32)
    w_rec = np.asarray(inp["L1_ffn_w_rec"], np.float32)
    sraw = np.zeros((128, 4), np.float32)
    for i, h in enumerate(heads):
        sraw[0:96, i] = w_std[h]
        sraw[96:104, i] = 1.0
        sraw[104:128, i] = w_std[h]
    d["sraw_in"] = sraw
    d["wrec8_in"] = np.ascontiguousarray(
        np.tile(w_rec[heads][None, :], (8, 1)))

    t = np.arange(T, dtype=np.float64)
    inv0 = 1.0 / 10000.0 ** (np.arange(0, 16, 2) / 16.0)
    f0 = t[None, :] * inv0[:, None]
    d["cos0_t"] = np.cos(f0).astype(np.float32)
    d["sin0_t"] = np.sin(f0).astype(np.float32)
    inv1 = 1.0 / 10000.0 ** (np.arange(0, 64, 2) / 64.0)
    f1 = t[None, :] * inv1[:, None]
    c1 = np.cos(f1).astype(np.float32)
    s1 = np.sin(f1).astype(np.float32)
    d["cos1_t"] = c1
    d["sin1_t"] = s1
    d["cos1o_t"] = np.ascontiguousarray(c1[:, j * TOK:(j + 1) * TOK])
    d["sin1o_t"] = np.ascontiguousarray(s1[:, j * TOK:(j + 1) * TOK])

    m = np.zeros((128, 4 * 512), np.float32)
    for r in range(4):
        for i in range(128):
            qq = np.arange(512)
            m[i, r * 512: (r + 1) * 512] = (qq >= 128 * r + i).astype(
                np.float32)
    d["masks_in"] = m
    return d


def kernel(**inputs):
    nc = build_nc(stage=99, debug=False)
    in_maps = [make_inputs(inputs, c) for c in range(8)]
    res = run_bass_kernel_spmd(nc, in_maps, core_ids=list(range(8)))
    out = np.empty((2, T, D), dtype=np.float32)
    for c in range(8):
        b, j = c // 4, c % 4
        out[b, j * TOK:(j + 1) * TOK, :] = res.results[c]["xout"].T
    return out
